# version 02 (final)
# baseline (speedup 1.0000x reference)
import numpy as np
import ml_dtypes
from contextlib import ExitStack

import concourse.bass as bass
import concourse.mybir as mybir
from concourse.bass_utils import run_bass_kernel_spmd

F32 = mybir.dt.float32
BF16 = mybir.dt.bfloat16
AF = mybir.ActivationFunctionType
ALU = mybir.AluOpType
AXX = mybir.AxisListType.X

D = 1024
T = 4096
EXPERTS = [8, 0, 1, 2, 3, 4, 5, 6, 7]
import os as _os
if _os.environ.get("KERN_NEXP"):
    EXPERTS = EXPERTS[: int(_os.environ["KERN_NEXP"])]
DC = 8
TC = 8
KTOP = 65536.0
EPS = 1e-5
N_BIS = 34

ENGINES = ["sync", "tensor", "vector", "scalar", "gpsimd"]


class Prog:
    def __init__(self):
        self.ops = {e: [] for e in ENGINES}

    def on(self, eng, fn):
        self.ops[eng].append(fn)

    def realize(self, nc):
        with nc.Block() as blk:
            for name in ENGINES:
                fns = self.ops[name]
                if not fns:
                    continue

                def body(eng, fns=fns):
                    for fn in fns:
                        fn(eng)

                getattr(blk, name)(body)


class Ctr:
    def __init__(self, handle, step=1):
        self.h = handle
        self.step = step
        self.n = 0


def build(parts="fag"):
    nc = bass.Bass("TRN2", target_bir_lowering=False, detect_race_conditions=False)

    xT = nc.dram_tensor("xT", [D, T], F32, kind="ExternalInput").ap()
    w1 = nc.dram_tensor("w1", [8, D, 4096], F32, kind="ExternalInput").ap()
    w2 = nc.dram_tensor("w2", [8, 4096, D], F32, kind="ExternalInput").ap()
    ws1 = nc.dram_tensor("ws1", [D, 4096], F32, kind="ExternalInput").ap()
    ws2 = nc.dram_tensor("ws2", [4096, D], F32, kind="ExternalInput").ap()
    gwTd = nc.dram_tensor("gwT", [D, 8], F32, kind="ExternalInput").ap()
    b1cd = nc.dram_tensor("b1c", [128, 9 * 32], F32, kind="ExternalInput").ap()
    b2rd = nc.dram_tensor("b2r", [8, D], F32, kind="ExternalInput").ap()
    bs2d = nc.dram_tensor("bs2r", [1, D], F32, kind="ExternalInput").ap()
    wqkvT = nc.dram_tensor("wqkvT", [D, 3 * D], F32, kind="ExternalInput").ap()
    wprojT = nc.dram_tensor("wprojT", [D, D], F32, kind="ExternalInput").ap()
    outd = nc.dram_tensor("out", [D, T], F32, kind="ExternalOutput").ap()
    cc_in = nc.dram_tensor("cc_in", [8, T], F32).ap()
    cc_out = nc.dram_tensor("cc_out", [64, T], F32).ap()

    p = Prog()
    st = ExitStack()
    e = st.enter_context

    X = e(nc.sbuf_tensor("X", [128, DC * T], BF16))
    AR = e(nc.sbuf_tensor("AR", [128, 32768], BF16))
    ACC = e(nc.sbuf_tensor("ACC", [128, 16 * 1024], BF16))
    HSB = e(nc.sbuf_tensor("HSB", [128, 8 * 512], BF16))
    SQ = e(nc.sbuf_tensor("SQ", [128, 1024], BF16))
    T1 = e(nc.sbuf_tensor("T1", [128, 512], F32))
    U = e(nc.sbuf_tensor("U", [128, 3076], F32))
    RWB = e(nc.sbuf_tensor("RWB", [128, T], BF16))
    SMF = e(nc.sbuf_tensor("SMF", [128, 1040], F32))
    SMB = e(nc.sbuf_tensor("SMB", [128, 4240], BF16))
    GCOL = e(nc.sbuf_tensor("GCOL", [128, 32 * 8], F32))
    B1C = e(nc.sbuf_tensor("B1C", [128, 9 * 32], F32))
    GWTB = e(nc.sbuf_tensor("GWTB", [128, 64], BF16))
    IDNB = e(nc.sbuf_tensor("IDNB", [128, 128], BF16))

    GATH = U[:, 0:2048]
    MSK = U[:, 2048:3072]
    OSTG = [U[:, 2048:2560], U[:, 2560:3072]]
    AT32 = U[:, 2048:2304]
    NMX = U[:, 3072:3073]
    RCP = U[:, 3073:3074]
    RCP2 = U[:, 3074:3075]
    RLN = U[:, 3075:3076]
    MURB = RWB[0:1, :]
    GSH = RWB[0:2, :]
    RSTDB = RWB[32:33, :]
    SIGB = RWB[64:65, :]
    ROWA = SMF[32:33, 0:512]
    ROWB = SMF[32:33, 512:1024]
    ONS8C = SMF[0:8, 392:393]
    ONR8 = SMF[32:33, 1032:1040]
    EPSC = SMF[32:33, 1030:1031]
    ONESRF = SMF[0:1, 0:128]
    TOTSB = SMF[0:1, 128:129]
    THC = SMF[:, 1024:1025]
    BSD = SMF[:, 1026:1027]
    BSG = SMF[:, 1027:1028]
    CNT2 = SMF[:, 1028:1030]
    ONESCF = SMF[:, 1031:1032]
    EXPS = U[0:8, 0:512]
    GEXL = SMB[0:8, 0:2048]
    GEXH = SMB[32:40, 0:2048]
    BEXL = SMB[0:8, 2048:3072]
    BEXH = SMB[32:40, 2048:3072]
    BSH = SMB[0:2, 3072:4096]
    ONESRB0 = SMB[0:1, 4096:4224]
    ONESRB32 = SMB[32:33, 3072:3200]
    ONESRB64 = SMB[64:65, 3072:3200]
    IDN8BL = SMB[0:8, 4224:4232]
    IDN8BH = SMB[32:40, 3200:3208]
    ONECB = SMB[:, 4232:4233]
    SRWR = SMB[0:1, 0:256]

    PSH = e(nc.psum_tensor("PSH", [128, 1024], F32))
    PSY = e(nc.psum_tensor("PSY", [128, 2048], F32))
    PSB = e(nc.psum_tensor("PSB", [128, 2048], BF16))

    dS = Ctr(e(nc.semaphore("dS")), 16)
    dG = Ctr(e(nc.semaphore("dG")), 16)
    cc = Ctr(e(nc.semaphore("cc")), 1)
    cT = Ctr(e(nc.semaphore("cT")), 1)
    cV = Ctr(e(nc.semaphore("cV")), 1)
    cS = Ctr(e(nc.semaphore("cS")), 1)

    def op(eng, fn, inc=None):
        if inc is None:
            p.on(eng, fn)
            return None
        inc.n += inc.step
        tgt = inc.n

        def wrapped(e_, fn=fn, inc=inc):
            fn(e_).then_inc(inc.h, inc.step)

        p.on(eng, wrapped)
        return tgt

    def wait(eng, ctr, tgt):
        if tgt and tgt > 0:
            p.on(eng, lambda e_, c=ctr, t=tgt: e_.wait_ge(c.h, t))

    def dma(eng, out_, in_, **kw):
        c = dG if eng == "gpsimd" else dS
        return op(eng, lambda e_, o=out_, i=in_, kw=kw: e_.dma_start(out=o, in_=i, **kw), c)

    def xbar(engs=("tensor", "vector", "scalar")):
        tv, tc, ts_ = cV.n, cT.n, cS.n
        for eng in engs:
            if eng != "vector":
                wait(eng, cV, tv)
            if eng != "tensor":
                wait(eng, cT, tc)
            if eng != "scalar":
                wait(eng, cS, ts_)

    t_x = 0
    for dc in range(DC):
        t_x = dma("gpsimd", X[:, dc * T:(dc + 1) * T], xT[dc * 128:(dc + 1) * 128, :])
    t_b1 = dma("sync", B1C[:, :], b1cd)
    for k in range(8):
        t_gw = dma("sync", U[:, k * 8:(k + 1) * 8], gwTd[k * 128:(k + 1) * 128, :])
    t_bex = dma("gpsimd", BEXL, b2rd)
    t_bex = dma("gpsimd", BEXH, b2rd)
    t_bsh = dma("gpsimd", BSH[1:2, :], bs2d)

    tg = op("gpsimd", lambda e_: e_.iota(T1[:, 0:128], [[1, 128]], channel_multiplier=0,
                                         allow_small_or_imprecise_dtypes=True), cT)
    tg = op("gpsimd", lambda e_: e_.iota(T1[:, 128:256], [[0, 128]], channel_multiplier=1,
                                         allow_small_or_imprecise_dtypes=True), cT)
    tg = op("gpsimd", lambda e_: e_.memset(ONECB, 1.0), cT)
    tg = op("gpsimd", lambda e_: e_.memset(ONESRB0, 1.0), cT)
    tg = op("gpsimd", lambda e_: e_.memset(ONESRB32, 1.0), cT)
    tg = op("gpsimd", lambda e_: e_.memset(ONESRB64, 1.0), cT)
    tg = op("gpsimd", lambda e_: e_.memset(ONS8C, 1.0), cT)
    tg = op("gpsimd", lambda e_: e_.memset(EPSC, EPS), cT)
    tg = op("gpsimd", lambda e_: e_.memset(ONESRF, 1.0), cT)
    tg = op("gpsimd", lambda e_: e_.memset(ONESCF, 1.0), cT)
    tg = op("gpsimd", lambda e_: e_.memset(ONR8, 1.0), cT)
    tg = op("gpsimd", lambda e_: e_.memset(RWB[0:2, :], 1.0), cT)
    wait("gpsimd", dG, t_bsh)
    tg = op("gpsimd", lambda e_: e_.memset(SMB[0:1, 3072:4096], 2.0), cT)
    t_initG = tg
    wait("vector", cT, t_initG)
    tv = op("vector", lambda e_: e_.tensor_tensor(IDNB[:, :], T1[:, 0:128],
                                                  T1[:, 128:256], ALU.is_equal), cV)
    tv = op("vector", lambda e_: e_.tensor_tensor(IDN8BL, T1[0:8, 0:8],
                                                  T1[0:8, 128:136], ALU.is_equal), cV)
    wait("sync", cV, tv)
    t_i8h = dma("sync", IDN8BH, IDN8BL)
    wait("vector", dS, t_gw)
    tv = op("vector", lambda e_: e_.tensor_copy(GWTB[:, :], U[:, 0:64]), cV)
    t_initV = tv

    wait("tensor", dG, t_x)
    wait("tensor", cV, t_initV)
    wait("tensor", cT, t_initG)
    wait("tensor", dS, t_b1)
    wait("scalar", dG, t_x)
    wait("vector", dG, t_x)

    wstate = {"g": 0, "rel": {}, "loadtgt": {}}

    def wq_aps(par):
        o = par * 16384
        return AR[:, o:o + 8192], AR[:, o + 8192:o + 16384]

    def load_quarter(eidx, q, pre_wait_cT=0):
        g = wstate["g"]
        wstate["g"] += 1
        if g >= 2:
            wait("gpsimd", cT, wstate["rel"][g - 2])
        if pre_wait_cT:
            wait("gpsimd", cT, pre_wait_cT)
        a1, a2 = wq_aps(g % 2)
        if eidx == 8:
            s1 = ws1[:, q * 1024:(q + 1) * 1024]
            s2 = ws2[q * 1024:(q + 1) * 1024, :]
        else:
            s1 = w1[eidx, :, q * 1024:(q + 1) * 1024]
            s2 = w2[eidx, q * 1024:(q + 1) * 1024, :]
        t = 0
        for k in range(8):
            t = dma("gpsimd", a1[:, k * 1024:(k + 1) * 1024], s1[k * 128:(k + 1) * 128, :])
        for k in range(8):
            t = dma("gpsimd", a2[:, k * 1024:(k + 1) * 1024], s2[k * 128:(k + 1) * 128, :])
        wstate["loadtgt"][g] = t
        return g

    gstate = {"t_gateprep_V": 0, "t_gateprep_T": 0}

    def serial_bisect_and_gate(t_gathS):
        op("vector", lambda e_: e_.memset(THC, 0.5), cV)
        t_ms = cV.n
        wait("vector", dS, t_gathS)
        wait("vector", cV, t_ms)
        for i in range(N_BIS):
            delta = 0.25 * (0.5 ** i)
            op("vector", lambda e_: e_.tensor_scalar(
                MSK, GATH[:, 0:1024], THC, None, ALU.is_ge), None)
            op("vector", lambda e_: e_.tensor_reduce(
                CNT2[:, 0:1], MSK, AXX, ALU.add), None)
            op("vector", lambda e_: e_.tensor_scalar(
                MSK, GATH[:, 1024:2048], THC, None, ALU.is_ge), None)
            op("vector", lambda e_: e_.tensor_reduce(
                CNT2[:, 1:2], MSK, AXX, ALU.add), None)
            t1 = op("vector", lambda e_: e_.drain(), cV)
            wait("tensor", cV, t1)
            tm1 = op("tensor", lambda e_: e_.matmul(
                PSH[0:1, 0:2], ONESCF, CNT2, start=True, stop=True), cT)
            wait("vector", cT, tm1)
            op("vector", lambda e_: e_.tensor_reduce(
                TOTSB, PSH[0:1, 0:2], AXX, ALU.add), None)
            t2 = op("vector", lambda e_: e_.drain(), cV)
            wait("tensor", cV, t2)
            tm2 = op("tensor", lambda e_: e_.matmul(
                PSH[:, 512:513], ONESRF, TOTSB, start=True, stop=True), cT)
            wait("vector", cT, tm2)
            t3 = op("vector", lambda e_: e_.tensor_scalar(
                BSD, PSH[:, 512:513], KTOP - 0.5, None, ALU.subtract), cV)
            wait("scalar", cV, t3)
            t4 = op("scalar", lambda e_: e_.activation(BSG, BSD, AF.Sign), cS)
            wait("vector", cS, t4)
            t5 = op("vector", lambda e_, d=delta: e_.scalar_tensor_tensor(
                THC, BSG, d, THC, ALU.mult, ALU.add), cV)
            wait("vector", cV, t5)
        op("vector", lambda e_: e_.scalar_tensor_tensor(
            GEXL, GEXL, THC[0:8, 0:1], GEXL, ALU.is_ge, ALU.mult), cV)
        op("vector", lambda e_: e_.scalar_tensor_tensor(
            GEXH, GEXH, THC[32:40, 0:1], GEXH, ALU.is_ge, ALU.mult), cV)
        op("vector", lambda e_: e_.tensor_scalar(GEXL, GEXL, 0.5, None, ALU.mult), cV)
        t = op("vector", lambda e_: e_.tensor_scalar(GEXH, GEXH, 0.5, None, ALU.mult), cV)
        tg_ = op("vector", lambda e_: e_.drain(), cV)
        wait("tensor", cV, tg_)
        gv = {}
        ttr = 0
        for tt in range(32):
            pb = PSB[:, (tt % 2) * 1024:(tt % 2) * 1024 + 8]
            if tt >= 2:
                wait("tensor", cV, gv[tt - 2])
            if tt < 16:
                src = GEXL[:, tt * 128:(tt + 1) * 128]
                idn = IDN8BL
            else:
                src = GEXH[:, (tt - 16) * 128:(tt - 15) * 128]
                idn = IDN8BH
            ttr = op("tensor", lambda e_, src=src, idn=idn, pb=pb: e_.transpose(
                pb, src, idn), cT)
            wait("vector", cT, ttr)
            gv[tt] = op("vector", lambda e_, tt=tt, pb=pb: e_.tensor_copy(
                GCOL[:, tt * 8:(tt + 1) * 8], pb), cV)
        tdone = op("vector", lambda e_: e_.drain(), cV)
        gstate["t_gateprep_V"] = tdone
        gstate["t_gateprep_T"] = ttr

    def ln_pass(with_gate, pre_T=0, pre_V=0):
        res = {"xn": [], "t_ccS": 0, "last_V": 0, "m2": {}}
        if pre_V:
            wait("tensor", cV, pre_V)
            wait("scalar", cV, pre_V)
        if pre_T:
            wait("scalar", cT, pre_T)
            wait("vector", cT, pre_T)
        for c in range(TC):
            cs = slice(c * 512, (c + 1) * 512)
            if c > 0:
                wait("tensor", cV, res["last_V"])
            for dc in range(DC):
                prev = res["m2"].get(dc - 2 if dc >= 2 else dc + 6, 0)
                if prev:
                    wait("scalar", cT, prev)
                tsq = op("scalar", lambda e_, dc=dc, c=c: e_.activation(
                    SQ[:, (dc % 2) * 512:(dc % 2) * 512 + 512],
                    X[:, dc * T + c * 512: dc * T + (c + 1) * 512], AF.Square), cS)
                wait("tensor", cS, tsq)
                res["m2"][dc] = op("tensor", lambda e_, dc=dc: e_.matmul(
                    PSH[0:1, 512:1024], ONECB,
                    SQ[:, (dc % 2) * 512:(dc % 2) * 512 + 512],
                    start=(dc == 0), stop=(dc == DC - 1)), cT)
            for dc in range(DC):
                tmu = op("tensor", lambda e_, dc=dc, c=c: e_.matmul(
                    PSH[0:1, 0:512], ONECB,
                    X[:, dc * T + c * 512: dc * T + (c + 1) * 512],
                    start=(dc == 0), stop=(dc == DC - 1)), cT)
            wait("vector", cT, tmu)
            op("vector", lambda e_, cs=cs: e_.tensor_scalar(
                MURB[0:1, cs], PSH[0:1, 0:512], 1.0 / 1024.0, None, ALU.mult), cV)
            tmr = op("vector", lambda e_: e_.tensor_scalar(
                ROWA, PSH[0:1, 512:1024], 1.0 / 1024.0, None, ALU.mult), cV)
            wait("scalar", cV, tmr)
            tss = op("scalar", lambda e_, cs=cs: e_.activation(
                ROWB, MURB[0:1, cs], AF.Square), cS)
            wait("vector", cS, tss)
            tvv = op("vector", lambda e_: e_.tensor_tensor(
                ROWA, ROWA, ROWB, ALU.subtract), cV)
            wait("scalar", cV, tvv)
            op("scalar", lambda e_: e_.activation(ROWB, ROWA, AF.Ln, bias=EPSC), cS)
            op("scalar", lambda e_, cs=cs: e_.activation(
                RSTDB[0:1, cs], ROWB, AF.Exp, scale=-0.5), cS)
            tsg = op("scalar", lambda e_, cs=cs: e_.activation(
                SIGB[0:1, cs], ROWB, AF.Exp, scale=0.5), cS)
            wait("tensor", cS, tsg)
            op("tensor", lambda e_, cs=cs: e_.matmul(
                PSH[:, 0:512], ONESRB0, MURB[0:1, cs], start=True, stop=True), None)
            tbc = op("tensor", lambda e_, cs=cs: e_.matmul(
                PSH[:, 512:1024], ONESRB32, RSTDB[0:1, cs], start=True, stop=True), cT)
            wait("vector", cT, tbc)
            for dc in range(DC):
                xs = slice(dc * T + c * 512, dc * T + (c + 1) * 512)
                op("vector", lambda e_, xs=xs: e_.tensor_tensor(
                    T1[:, 0:512], X[:, xs], PSH[:, 0:512], ALU.subtract), None)
                op("vector", lambda e_, xs=xs: e_.tensor_tensor(
                    X[:, xs], T1[:, 0:512], PSH[:, 512:1024], ALU.mult),
                    cV if dc == DC - 1 else None)
            t_xn = cV.n
            res["xn"].append(t_xn)
            res["last_V"] = t_xn
            if with_gate:
                wait("tensor", cV, t_xn)
                for dc in range(DC):
                    tlg = op("tensor", lambda e_, dc=dc, c=c: e_.matmul(
                        PSH[0:8, 0:512], GWTB[:, dc * 8:(dc + 1) * 8],
                        X[:, dc * T + c * 512: dc * T + (c + 1) * 512],
                        start=(dc == 0), stop=(dc == DC - 1)), cT)
                wait("scalar", cT, tlg)
                tex = op("scalar", lambda e_: e_.activation(
                    EXPS, PSH[0:8, 0:512], AF.Exp), cS)
                wait("tensor", cS, tex)
                tz = op("tensor", lambda e_: e_.matmul(
                    PSH[0:1, 512:1024], ONS8C, EXPS, start=True, stop=True), cT)
                wait("vector", cT, tz)
                tzr = op("vector", lambda e_: e_.reciprocal(ROWA, PSH[0:1, 512:1024]), cV)
                wait("tensor", cV, tzr)
                tzb = op("tensor", lambda e_: e_.matmul(
                    PSH[0:8, 512:1024], ONR8, ROWA, start=True, stop=True), cT)
                wait("vector", cT, tzb)
                op("vector", lambda e_: e_.tensor_tensor(
                    EXPS, EXPS, PSH[0:8, 512:1024], ALU.mult), cV)
                gdst = (GEXL[:, (c % 4) * 512:(c % 4) * 512 + 512] if c < 4
                        else GEXH[:, (c - 4) * 512:(c - 4) * 512 + 512])
                tbf = op("vector", lambda e_, gdst=gdst: e_.tensor_copy(gdst, EXPS), cV)
                res["last_V"] = tbf
                wait("sync", cV, tbf)
                res["t_ccS"] = dma("sync", cc_in[:, cs], EXPS)
        return res

    def ffn_sublayer(final, pre_T=0, pre_V=0, pre_arena_T=0):
        ln = ln_pass(with_gate=True, pre_T=pre_T, pre_V=pre_V)
        wait("gpsimd", dS, ln["t_ccS"])
        t_ccg = op("gpsimd", lambda e_: e_.collective_compute(
            "AllGather", ALU.bypass, replica_groups=[list(range(8))],
            ins=[cc_in.opt()], outs=[cc_out.opt()]), cc)
        wait("sync", cc, t_ccg)
        t_gath = dma("sync", GATH, cc_out.rearrange("r (h m) -> (r h) m", h=2))
        serial_bisect_and_gate(t_gath)

        ycnt = {"k": 0, "evV": {}, "gelS": {}}
        t_recV = 0
        first_load = True
        t_pre_prev = {}
        for half in range(2):
            hc0 = half * 4
            if half == 1:
                wait("tensor", cV, t_recV)
            wait("tensor", cV, gstate["t_gateprep_V"])
            if first_load:
                wait("tensor", dG, t_bex)
            t_pre_last = 0
            for tt16 in range(16):
                tglob = half * 16 + tt16
                if tglob < 16:
                    glh = GEXL[:, tglob * 128:(tglob + 1) * 128]
                    blh = BEXL
                else:
                    glh = GEXH[:, (tglob - 16) * 128:(tglob - 15) * 128]
                    blh = BEXH
                pp = PSY[:, (tt16 % 2) * 1024:(tt16 % 2) * 1024 + 1024]
                if tt16 >= 2:
                    wait("tensor", cV, t_pre_prev[tt16 - 2])
                for dh in range(2):
                    op("tensor", lambda e_, glh=glh, blh=blh, pp=pp, dh=dh:
                       e_.matmul(pp[:, dh * 512:(dh + 1) * 512],
                                 glh, blh[:, dh * 512:(dh + 1) * 512],
                                 start=True, stop=True),
                       cT if dh == 1 else None)
                tppT = cT.n
                wait("vector", cT, tppT)
                t_pre_last = op("vector", lambda e_, tt16=tt16, pp=pp:
                                e_.tensor_copy(
                                    ACC[:, tt16 * 1024:(tt16 + 1) * 1024], pp),
                                cV)
                t_pre_prev[tt16] = t_pre_last
            ycnt["evV"][ycnt["k"] - 2] = t_pre_last
            ycnt["evV"][ycnt["k"] - 1] = t_pre_last
            for ei, eidx in enumerate(EXPERTS):
                for q in range(4):
                    g = load_quarter(eidx, q,
                                     pre_wait_cT=pre_arena_T if first_load else 0)
                    first_load = False
                    par = g % 2
                    a1, a2 = wq_aps(par)
                    wait("tensor", dG, wstate["loadtgt"][g])
                    if ei == 0 and q == 0:
                        wait("tensor", cV, ln["last_V"])
                        wait("tensor", cV, gstate["t_gateprep_V"])
                        if half == 1:
                            wait("tensor", cV, t_recV)
                    for ci in range(4):
                        c = hc0 + ci
                        for fb in range(8):
                            ph = PSH[:, (fb % 2) * 512:(fb % 2) * 512 + 512]
                            prevgel = ycnt["gelS"].get((fb - 2) % 8, 0)
                            if prevgel:
                                wait("tensor", cS, prevgel)
                            for dk in range(DC):
                                op("tensor", lambda e_, dk=dk, fb=fb, c=c, a1=a1, ph=ph:
                                   e_.matmul(
                                       ph,
                                       a1[:, dk * 1024 + fb * 128: dk * 1024 + (fb + 1) * 128],
                                       X[:, dk * T + c * 512: dk * T + (c + 1) * 512],
                                       start=(dk == 0), stop=(dk == DC - 1)),
                                   cT if dk == DC - 1 else None)
                            th = cT.n
                            wait("scalar", cT, th)
                            bcol = B1C[:, eidx * 32 + q * 8 + fb: eidx * 32 + q * 8 + fb + 1]
                            ycnt["gelS"][fb] = op(
                                "scalar", lambda e_, fb=fb, ph=ph, bcol=bcol:
                                e_.activation(HSB[:, fb * 512:(fb + 1) * 512], ph,
                                              AF.Gelu_apprx_tanh, bias=bcol), cS)
                        t_gel_all = ycnt["gelS"][7]
                        for tt in range(4):
                            k = ycnt["k"]
                            ycnt["k"] += 1
                            yb = PSY[:, (k % 2) * 1024:(k % 2) * 1024 + 1024]
                            tglob = half * 16 + ci * 4 + tt
                            prevev = ycnt["evV"].get(k - 2, 0)
                            if prevev:
                                wait("tensor", cV, prevev)
                            if tt == 0:
                                wait("tensor", cS, t_gel_all)
                            first = True
                            if q == 0 and eidx == 8:
                                for dh in range(2):
                                    op("tensor", lambda e_, tglob=tglob, yb=yb, dh=dh:
                                       e_.matmul(
                                           yb[:, dh * 512:(dh + 1) * 512],
                                           GSH[:, tglob * 128:(tglob + 1) * 128],
                                           BSH[:, dh * 512:(dh + 1) * 512],
                                           start=True, stop=False), None)
                                first = False
                            for fb in range(8):
                                for dh in range(2):
                                    op("tensor", lambda e_, fb=fb, tt=tt, dh=dh, a2=a2,
                                       yb=yb, first=first:
                                       e_.matmul(
                                           yb[:, dh * 512:(dh + 1) * 512],
                                           HSB[:, fb * 512 + tt * 128: fb * 512 + (tt + 1) * 128],
                                           a2[:, fb * 1024 + dh * 512: fb * 1024 + (dh + 1) * 512],
                                           start=(first and fb == 0), stop=(fb == 7)),
                                       cT if (fb == 7 and dh == 1) else None)
                            ty = cT.n
                            wait("vector", cT, ty)
                            atile = ACC[:, (ci * 4 + tt) * 1024:(ci * 4 + tt + 1) * 1024]
                            if eidx == 8:
                                tev = op("vector", lambda e_, yb=yb, atile=atile:
                                         e_.scalar_tensor_tensor(
                                             atile, yb, 0.5, atile,
                                             ALU.mult, ALU.add), cV)
                            else:
                                gc = GCOL[:, tglob * 8 + eidx: tglob * 8 + eidx + 1]
                                tev = op("vector", lambda e_, yb=yb, atile=atile, gc=gc:
                                         e_.scalar_tensor_tensor(
                                             atile, yb, gc, atile,
                                             ALU.mult, ALU.add), cV)
                            ycnt["evV"][k] = tev
                    wstate["rel"][g] = cT.n
            t_recV = recombine_half(half, final, ycnt)
        return cT.n, t_recV

    def recombine_half(half, final, ycnt):
        last_ev = ycnt["evV"][ycnt["k"] - 1]
        wait("tensor", cV, last_ev)
        for tt in range(16):
            tglob = half * 16 + tt
            op("tensor", lambda e_, tt=tt, tglob=tglob: e_.matmul(
                PSY[:, tt * 128:(tt + 1) * 128], ONESRB64,
                SIGB[0:1, tglob * 128:(tglob + 1) * 128], start=True, stop=True),
                cT if tt == 15 else None)
        t_sig = cT.n
        wait("vector", cT, t_sig)
        trv = {}
        ost_dma = {0: 0, 1: 0}
        t_last = 0
        for dc in range(DC):
            for tt in range(16):
                kk = dc * 16 + tt
                pb = PSB[:, (kk % 2) * 1024:(kk % 2) * 1024 + 128]
                if kk >= 2:
                    wait("tensor", cV, trv[kk - 2])
                ttr = op("tensor", lambda e_, tt=tt, dc=dc, pb=pb: e_.transpose(
                    pb, ACC[:, tt * 1024 + dc * 128: tt * 1024 + (dc + 1) * 128],
                    IDNB[:, :]), cT)
                wait("vector", cT, ttr)
                xs = slice(dc * T + half * 2048 + tt * 128,
                           dc * T + half * 2048 + (tt + 1) * 128)
                op("vector", lambda e_, xs=xs, tt=tt: e_.tensor_tensor(
                    T1[:, 0:128], X[:, xs], PSY[:, tt * 128:(tt + 1) * 128],
                    ALU.mult), None)
                if final:
                    buf = (tt // 4) % 2
                    oc = (tt % 4) * 128
                    if tt % 4 == 0 and ost_dma[buf]:
                        wait("vector", dS, ost_dma[buf])
                    tvx = op("vector", lambda e_, pb=pb, buf=buf, oc=oc: e_.tensor_tensor(
                        OSTG[buf][:, oc:oc + 128], T1[:, 0:128], pb, ALU.add), cV)
                    trv[kk] = tvx
                    if tt % 4 == 3:
                        wait("sync", cV, tvx)
                        ost_dma[buf] = dma(
                            "sync",
                            outd[dc * 128:(dc + 1) * 128,
                                 half * 2048 + (tt - 3) * 128: half * 2048 + (tt + 1) * 128],
                            OSTG[buf][:, 0:512])
                else:
                    tvx = op("vector", lambda e_, pb=pb, xs=xs: e_.tensor_tensor(
                        X[:, xs], T1[:, 0:128], pb, ALU.add), cV)
                    trv[kk] = tvx
                t_last = trv[kk]
        return t_last

    def attn_sublayer(final, pre_T, pre_V):
        ln = ln_pass(with_gate=False, pre_T=pre_T, pre_V=pre_V)
        Q2 = ACC[:, 0:4096]
        K2 = ACC[:, 4096:8192]
        VR = [ACC[:, 8192:10240], ACC[:, 10240:12288]]
        QR = ACC[:, 12288:14336]
        KR = ACC[:, 14336:16384]
        ATN = [SQ[:, 0:256], SQ[:, 256:512]]
        ATNT = [SQ[:, 512:768], SQ[:, 768:1024]]

        t_qrV = 0
        t_apreV = 0
        t_outT = 0
        t_nrmV = 0
        for pair in range(8):
            if pair == 0:
                wait("gpsimd", cT, wstate["rel"][wstate["g"] - 1])
            else:
                wait("gpsimd", cT, t_outT)
            tload = 0
            for dk in range(DC):
                for sec in range(3):
                    tload = dma("gpsimd",
                                HSB[:, dk * 384 + sec * 128: dk * 384 + (sec + 1) * 128],
                                wqkvT[dk * 128:(dk + 1) * 128,
                                      sec * 1024 + pair * 128: sec * 1024 + (pair + 1) * 128])
            wait("tensor", dG, tload)
            if pair == 0:
                wait("tensor", cV, ln["last_V"])
            else:
                wait("tensor", cV, max(t_qrV, t_nrmV))
            tq2 = {}
            for qk, dst in ((0, Q2), (1, K2)):
                for pc in range(8):
                    ph = PSH[:, (pc % 2) * 512:(pc % 2) * 512 + 512]
                    prev = tq2.get((qk, pc - 2)) if pc >= 2 else (
                        tq2.get((0, 6 + pc)) if qk == 1 else None)
                    if prev:
                        wait("tensor", cV, prev)
                    for dk in range(DC):
                        op("tensor", lambda e_, dk=dk, pc=pc, qk=qk, ph=ph: e_.matmul(
                            ph, HSB[:, dk * 384 + qk * 128: dk * 384 + qk * 128 + 128],
                            X[:, dk * T + pc * 512: dk * T + (pc + 1) * 512],
                            start=(dk == 0), stop=(dk == DC - 1)),
                            cT if dk == DC - 1 else None)
                    tq = cT.n
                    wait("vector", cT, tq)
                    tq2[(qk, pc)] = op("vector", lambda e_, dst=dst, pc=pc, ph=ph:
                                       e_.tensor_copy(dst[:, pc * 512:(pc + 1) * 512], ph),
                                       cV)
            xbar()
            t_vrV = 0
            for j in range(16):
                for xt in range(2):
                    kk = j * 2 + xt
                    ph = PSH[:, (kk % 2) * 512:(kk % 2) * 512 + 128]
                    if kk == 0:
                        if pair > 0:
                            wait("tensor", cV, t_apreV)
                        wait("tensor", cV, tq2[(1, 7)])
                    elif kk >= 2:
                        wait("tensor", cV, t_vrV)
                    for dk in range(DC):
                        op("tensor", lambda e_, dk=dk, j=j, xt=xt, ph=ph: e_.matmul(
                            ph,
                            X[:, dk * T + j * 256 + xt * 128: dk * T + j * 256 + (xt + 1) * 128],
                            HSB[:, dk * 384 + 256: dk * 384 + 384],
                            start=(dk == 0), stop=(dk == DC - 1)),
                            cT if dk == DC - 1 else None)
                    tvp = cT.n
                    wait("vector", cT, tvp)
                    for hh in range(2):
                        t_vrV = op("vector", lambda e_, hh=hh, j=j, xt=xt, ph=ph:
                                   e_.tensor_copy(
                                       VR[hh][:, xt * 1024 + j * 64: xt * 1024 + (j + 1) * 64],
                                       ph[:, hh * 64:(hh + 1) * 64]), cV)
            xbar()
            for hh in range(2):
                h = pair * 2 + hh
                wait("vector", cT, t_outT if (hh == 1) else 0)
                tasm = 0
                for m in range(8):
                    for jl in range(2):
                        sq_ = Q2[(hh % 2) * 64:(hh % 2) * 64 + 64,
                                 (2 * m + jl) * 256:(2 * m + jl + 1) * 256]
                        sk_ = K2[(hh % 2) * 64:(hh % 2) * 64 + 64,
                                 (2 * m + jl) * 256:(2 * m + jl + 1) * 256]
                        dq = QR[jl * 64:(jl + 1) * 64, m * 256:(m + 1) * 256]
                        dk_ = KR[jl * 64:(jl + 1) * 64, m * 256:(m + 1) * 256]
                        op("vector", lambda e_, s=sq_, d=dq: e_.tensor_scalar(
                            d, s, 0.125, None, ALU.mult), None)
                        tasm = op("vector", lambda e_, s=sk_, d=dk_: e_.tensor_copy(d, s),
                                  cV)
                t_qrV = tasm
                xbar()
                wait("tensor", cV, tasm)
                if hh == 0 and pair == 0:
                    wait("tensor", cV, t_vrV)
                tscT = {}
                for wt in range(2):
                    ps = PSY[:, wt * 512:wt * 512 + 256]
                    for m in range(8):
                        op("tensor", lambda e_, m=m, wt=wt, ps=ps: e_.matmul(
                            ps, QR[:, m * 256 + wt * 128: m * 256 + wt * 128 + 128],
                            KR[:, m * 256:(m + 1) * 256],
                            start=(m == 0), stop=(m == 7)),
                            cT if m == 7 else None)
                    tscT[wt] = cT.n
                xbar()
                for wt in range(2):
                    ps = PSY[:, wt * 512:wt * 512 + 256]
                    wait("vector", cT, tscT[wt])
                    if _os.environ.get("KERN_DEBUG") and h == int(_os.environ.get("KERN_DBGH", "15")) and wt == 0:
                        op("vector", lambda e_, ps=ps: e_.tensor_copy(
                            GCOL[:, 0:256], ps), cV)
                    tnm = op("vector", lambda e_, ps=ps: e_.tensor_reduce(
                        NMX, ps, AXX, ALU.max, negate=True), cV)
                    wait("vector", cV, tnm)
                    tsb = op("vector", lambda e_, ps=ps: e_.tensor_scalar(
                        T1[:, 0:256], ps, NMX, None, ALU.add), cV)
                    wait("scalar", cV, tsb)
                    tex = op("scalar", lambda e_: e_.activation(
                        AT32, T1[:, 0:256], AF.Exp), cS)
                    wait("vector", cS, tex)
                    if _os.environ.get("KERN_DEBUG") and h == int(_os.environ.get("KERN_DBGH", "15")) and wt == 0:
                        op("vector", lambda e_: e_.tensor_copy(B1C[:, 0:1], NMX), cV)
                        op("vector", lambda e_: e_.tensor_copy(
                            B1C[:, 4:260], AT32), cV)
                    tat = op("vector", lambda e_, wt=wt: e_.tensor_copy(
                        ATN[wt], AT32), cV)
                    wait("tensor", cV, tat)
                    for xt in range(2):
                        kb = (wt * 2 + xt) % 2
                        pb = PSB[:, kb * 1024:kb * 1024 + 128]
                        ttr = op("tensor", lambda e_, wt=wt, xt=xt, pb=pb: e_.transpose(
                            pb, ATN[wt][:, xt * 128:(xt + 1) * 128], IDNB[:, :]), cT)
                        wait("vector", cT, ttr)
                        op("vector", lambda e_, wt=wt, xt=xt, pb=pb: e_.tensor_copy(
                            ATNT[xt][:, wt * 128:(wt + 1) * 128], pb), cV)
                t_atV = cV.n
                xbar()
                wait("tensor", cV, t_atV)
                for xt in range(2):
                    op("tensor", lambda e_, xt=xt: e_.matmul(
                        PSH[0:1, 0:256], ONECB, ATNT[xt],
                        start=(xt == 0), stop=(xt == 1)),
                        cT if xt == 1 else None)
                wait("vector", cT, cT.n)
                def _recip(e_):
                    with nc.allow_low_precision(reason="1/rowsum as bf16 rhs"):
                        return e_.reciprocal(SRWR, PSH[0:1, 0:256])
                trp = op("vector", _recip, cV)
                wait("tensor", cV, trp)
                op("tensor", lambda e_: e_.matmul(
                    PSH[:, 512:768], ONESRB0, SRWR, start=True, stop=True), cT)
                wait("vector", cT, cT.n)
                t_nrmV = op("vector", lambda e_: e_.tensor_copy(
                    T1[:, 256:512], PSH[:, 512:768]), cV)
                if _os.environ.get("KERN_DEBUG") and h == int(_os.environ.get("KERN_DBGH", "15")):
                    op("vector", lambda e_: e_.tensor_copy(
                        B1C[0:1, 1:257], SRWR), cV)
                for fc in range(8):
                    po = PSY[:, 1024 + (fc % 2) * 512:1024 + (fc % 2) * 512 + 256]
                    if fc >= 2:
                        wait("tensor", cV, t_apreV)
                    for xt in range(2):
                        op("tensor", lambda e_, fc=fc, xt=xt, hh=hh, po=po: e_.matmul(
                            po, VR[hh][:, xt * 1024 + fc * 128: xt * 1024 + (fc + 1) * 128],
                            ATNT[xt], start=(xt == 0), stop=(xt == 1)),
                            cT if xt == 1 else None)
                    t_outT = cT.n
                    wait("vector", cT, t_outT)
                    for jl in range(2):
                        jj = 2 * fc + jl
                        t_apreV = op(
                            "vector", lambda e_, jl=jl, jj=jj, h=h, po=po:
                            e_.tensor_tensor(
                                AR[(h % 2) * 64:(h % 2) * 64 + 64,
                                   (h // 2) * 4096 + jj * 256:(h // 2) * 4096 + (jj + 1) * 256],
                                po[jl * 64:(jl + 1) * 64, :],
                                T1[jl * 64:(jl + 1) * 64, 256:512],
                                ALU.mult), cV)
        xbar()
        t_prV = {}
        ost_dma = {0: 0, 1: 0}
        t_last = 0
        for dhalf in range(2):
            wait("gpsimd", cT, t_outT if dhalf == 0 else cT.n)
            twp = 0
            for fk in range(8):
                twp = dma("gpsimd", HSB[:, fk * 512:(fk + 1) * 512],
                          wprojT[fk * 128:(fk + 1) * 128, dhalf * 512:(dhalf + 1) * 512])
            wait("tensor", dG, twp)
            if dhalf == 0:
                wait("tensor", cV, t_apreV)
            for dc4 in range(4):
                dc = dhalf * 4 + dc4
                for c in range(TC):
                    kk = dc4 * 8 + c
                    py = PSY[:, (kk % 2) * 512:(kk % 2) * 512 + 512]
                    ph = PSH[:, (kk % 2) * 512:(kk % 2) * 512 + 512]
                    if kk >= 2:
                        wait("tensor", cV, t_prV[kk - 2])
                    op("tensor", lambda e_, c=c, py=py: e_.matmul(
                        py, ONESRB0, MURB[0:1, c * 512:(c + 1) * 512],
                        start=True, stop=False), None)
                    for fk in range(8):
                        op("tensor", lambda e_, fk=fk, c=c, dc4=dc4, py=py: e_.matmul(
                            py,
                            HSB[:, fk * 512 + dc4 * 128: fk * 512 + dc4 * 128 + 128],
                            AR[:, fk * 4096 + c * 512: fk * 4096 + (c + 1) * 512],
                            start=False, stop=(fk == 7)),
                            cT if fk == 7 else None)
                    op("tensor", lambda e_, c=c, ph=ph: e_.matmul(
                        ph, ONESRB64, SIGB[0:1, c * 512:(c + 1) * 512],
                        start=True, stop=True), cT)
                    tpj = cT.n
                    wait("vector", cT, tpj)
                    xs = slice(dc * T + c * 512, dc * T + (c + 1) * 512)
                    op("vector", lambda e_, xs=xs, ph=ph: e_.tensor_tensor(
                        T1[:, 0:512], X[:, xs], ph, ALU.mult), None)
                    if final:
                        buf = kk % 2
                        if ost_dma[buf]:
                            wait("vector", dS, ost_dma[buf])
                        tvx = op("vector", lambda e_, buf=buf, py=py: e_.tensor_tensor(
                            OSTG[buf], T1[:, 0:512], py, ALU.add), cV)
                        t_prV[kk] = tvx
                        wait("sync", cV, tvx)
                        ost_dma[buf] = dma(
                            "sync", outd[dc * 128:(dc + 1) * 128, c * 512:(c + 1) * 512],
                            OSTG[buf])
                    else:
                        t_prV[kk] = op("vector", lambda e_, xs=xs, py=py: e_.tensor_tensor(
                            X[:, xs], T1[:, 0:512], py, ALU.add), cV)
                    t_last = t_prV[kk]
        return cT.n, t_last

    lastT, lastV = 0, 0
    arenaT = 0
    if "f" in parts:
        lastT, lastV = ffn_sublayer(final=(parts == "f"))
        lastT = cT.n
    if "a" in parts:
        lastT, lastV = attn_sublayer(final=(parts == "fa"), pre_T=lastT, pre_V=lastV)
        arenaT = lastT
    if "g" in parts:
        ffn_sublayer(final=True, pre_T=lastT, pre_V=lastV, pre_arena_T=arenaT)

    if _os.environ.get("KERN_DEBUG"):
        dbgf = nc.dram_tensor("dbgf", [128, 2308], F32, kind="ExternalOutput").ap()
        dbgb = nc.dram_tensor("dbgb", [128, 4096], BF16, kind="ExternalOutput").ap()
        wait("sync", cV, cV.n)
        wait("sync", cT, cT.n)
        dma("sync", dbgf[:, 0:1], THC, allow_slow_non_contiguous=True)
        dma("sync", dbgf[:, 1:3], CNT2, allow_slow_non_contiguous=True)
        dma("sync", dbgf[:, 3:259], GCOL[:, :])
        dma("sync", dbgf[:, 259:2307], GATH)
        dma("sync", dbgf[0:1, 2307:2308], TOTSB, allow_slow_non_contiguous=True)
        dma("sync", dbgb[0:8, 0:2048], GEXL)
        dma("sync", dbgb[32:40, 0:2048], GEXH)
        dbga = nc.dram_tensor("dbga", [128, 4096], BF16, kind="ExternalOutput").ap()
        dma("sync", dbga, ACC[:, 0:4096])
        dbgar = nc.dram_tensor("dbgar", [128, 32768], BF16, kind="ExternalOutput").ap()
        dma("sync", dbgar, AR[:, :])
        dbgsq = nc.dram_tensor("dbgsq", [128, 1024], BF16, kind="ExternalOutput").ap()
        dma("sync", dbgsq, SQ[:, :])
        dbgacc = nc.dram_tensor("dbgacc", [128, 16384], BF16, kind="ExternalOutput").ap()
        dma("sync", dbgacc, ACC[:, :])
        dbgu = nc.dram_tensor("dbgu", [128, 768], F32, kind="ExternalOutput").ap()
        dma("sync", dbgu[:, 0:256], GCOL[:, 0:256])
        dma("sync", dbgu[:, 256:544], B1C[:, 0:288])
    wait("sync", dS, dS.n)
    wait("gpsimd", dG, dG.n)

    p.realize(nc)
    st.close()
    return nc


_PERM = None


def _perm():
    global _PERM
    if _PERM is None:
        pp = np.arange(4096)
        _PERM = 16 * (pp % 256) + pp // 256
    return _PERM


_CACHE = {}


def kernel(L0_attn_x, L0_attn_Wqkv, L0_attn_Wproj, L0_ffn_gate_weight,
           L0_ffn_W1, L0_ffn_b1, L0_ffn_W2, L0_ffn_b2, L0_ffn_Ws1,
           L0_ffn_bs1, L0_ffn_Ws2, L0_ffn_bs2, parts="fag"):
    x = np.asarray(L0_attn_x, dtype=np.float32)
    perm = _perm()

    w1 = np.ascontiguousarray(np.asarray(L0_ffn_W1, dtype=np.float32))
    w2 = np.ascontiguousarray(np.asarray(L0_ffn_W2, dtype=np.float32))
    ws1 = np.ascontiguousarray(np.asarray(L0_ffn_Ws1, dtype=np.float32))
    ws2 = np.ascontiguousarray(np.asarray(L0_ffn_Ws2, dtype=np.float32))
    gwT = np.ascontiguousarray(np.asarray(L0_ffn_gate_weight, dtype=np.float32).T)
    b1all = np.concatenate([np.asarray(L0_ffn_b1, dtype=np.float32),
                            np.asarray(L0_ffn_bs1, dtype=np.float32)[None, :]], axis=0)
    b1c = np.ascontiguousarray(
        b1all.reshape(9, 32, 128).transpose(2, 0, 1).reshape(128, 288))
    b2r = np.ascontiguousarray(np.asarray(L0_ffn_b2, dtype=np.float32))
    bs2r = np.ascontiguousarray(np.asarray(L0_ffn_bs2, dtype=np.float32)[None, :])
    wqkvT_ = np.ascontiguousarray(np.asarray(L0_attn_Wqkv, dtype=np.float32).T)
    wprojT_ = np.ascontiguousarray(np.asarray(L0_attn_Wproj, dtype=np.float32).T)

    if parts not in _CACHE:
        _CACHE[parts] = build(parts)
    nc = _CACHE[parts]

    in_maps = []
    for b in range(8):
        xt = np.ascontiguousarray(x[b].T[:, perm])
        in_maps.append({
            "xT": xt, "w1": w1, "w2": w2, "ws1": ws1, "ws2": ws2,
            "gwT": gwT, "b1c": b1c, "b2r": b2r, "bs2r": bs2r,
            "wqkvT": wqkvT_, "wprojT": wprojT_,
        })
    res = run_bass_kernel_spmd(nc, in_maps, core_ids=list(range(8)))

    inv = np.empty(4096, dtype=np.int64)
    inv[perm] = np.arange(4096)
    outs = []
    for b in range(8):
        y = res.results[b]["out"]
        outs.append(y[:, inv].T)
    return np.stack(outs, axis=0).astype(np.float32)
